# version 02 (final)
# baseline (speedup 1.0000x reference)
import numpy as np
from contextlib import ExitStack

import concourse.bass as bass
import concourse.mybir as mybir
from concourse.bass_utils import run_bass_kernel_spmd

F32 = mybir.dt.float32
BF16 = mybir.dt.bfloat16
AF = mybir.ActivationFunctionType
ALU = mybir.AluOpType
AX = mybir.AxisListType

NCORES = 8
D = 1024
CD = 8
B = 32
S = 2048
EPS = 1e-5
T = 512



class Sem:
    def __init__(self, ctx, nc, name):
        self.h = ctx.enter_context(nc.semaphore(name))
        self.n = 0


class Prog:

    ENGINES = ("sync", "gpsimd", "tensor", "scalar", "vector")

    def __init__(self, nc):
        self.nc = nc
        self.ops = {e: [] for e in self.ENGINES}

    def on(self, eng, fn, waits=(), inc=None):
        if inc is not None:
            inc[0].n += inc[1]
        self.ops[eng].append((tuple(w for w in waits if w is not None), fn, inc))
        return inc[0].n if inc is not None else None

    def chain(self, eng, fns, sem, waits=()):
        first = True
        for fn in fns:
            w = list(waits) + ([] if first else []) if first else [(sem, sem.n)]
            self.on(eng, fn, waits=w, inc=(sem, 1))
            first = False
        return sem.n

    def dma(self, eng, out, in_, waits=(), sem=None, **kw):
        return self.on(eng, lambda e: e.dma_start(out=out, in_=in_, **kw),
                       waits=waits, inc=(sem, 16) if sem is not None else None)

    def replay(self, block):
        def mk(name):
            ops = self.ops[name]

            def run(eng):
                for waits, fn, inc in ops:
                    for (sem, val) in waits:
                        eng.wait_ge(sem.h, val)
                    inst = fn(eng)
                    if inc is not None:
                        inst.then_inc(inc[0].h, inc[1])
            return run

        block.sync(mk("sync"))
        block.gpsimd(mk("gpsimd"))
        block.tensor(mk("tensor"))
        block.scalar(mk("scalar"))
        block.vector(mk("vector"))


def selfsync(e, sem):
    def step(inst):
        sem.n += 1
        inst.then_inc(sem.h, 1)
        e.wait_ge(sem.h, sem.n)
        return inst
    return step


def hist(lst, k, sem):
    return (sem, lst[k]) if k >= 0 and k < len(lst) else None


def chunk_major(w):
    r, n = w.shape
    c = r // 128
    return np.ascontiguousarray(
        w.reshape(c, 128, n).transpose(1, 0, 2).reshape(128, c * n))


def col8(v):
    return np.ascontiguousarray(v.reshape(8, 128).T)


def xtile_ap(xdram, b, t0, tw):
    return xdram[b].rearrange("(c p) s -> p c s", p=128)[:, :, t0:t0 + tw]



def emit_ln_tile(p, sems, bufs, x_chunks, xn_chunks, x_ready, hists, tw=T,
                 extra=None, cp_extra=None):
    s_pe, s_act, s_dve = sems["pe"], sems["act"], sems["dve"]
    k = len(hists["norm"])
    par = k % 2
    nbf = len(bufs["xbf"])

    xbf, xsq = bufs["xbf"][k % nbf], bufs["xsq"][k % nbf]

    def cp(e):
        for c in range(CD):
            i = e.tensor_copy(xbf[:, c * tw:(c + 1) * tw], x_chunks[c])
        return i
    p.on("vector", cp,
         waits=[x_ready, hist(hists["stats"], k - nbf, s_pe), cp_extra],
         inc=(s_dve, 1))
    hists["cp"].append(s_dve.n)

    def sq(e):
        for c in range(CD):
            i = e.activation(xsq[:, c * tw:(c + 1) * tw], x_chunks[c], AF.Square)
        return i
    p.on("scalar", sq,
         waits=[x_ready, hist(hists["stats"], k - nbf, s_pe), cp_extra],
         inc=(s_act, 1))
    hists["sq"].append(s_act.n)

    st1, st2 = bufs["ps_stat"](par)

    def stats(e):
        for c in range(CD):
            e.matmul(st1, bufs["ones_col_bf"][:], xbf[:, c * tw:(c + 1) * tw],
                     start=(c == 0), stop=(c == CD - 1))
        for c in range(CD):
            i = e.matmul(st2, bufs["ones_col_bf"][:], xsq[:, c * tw:(c + 1) * tw],
                         start=(c == 0), stop=(c == CD - 1))
        return i
    p.on("tensor", stats,
         waits=[(s_dve, hists["cp"][k]), (s_act, hists["sq"][k]),
                hist(hists["rows"], k - bufs.get("stat_par", 2), s_dve), extra],
         inc=(s_pe, 1))
    hists["stats"].append(s_pe.n)

    rr = bufs["rows"]
    mean = rr[0:1, 0:tw]
    slot_s = rr[0:1, tw:2 * tw]
    slot_t = rr[0:1, 2 * tw:3 * tw]

    def rows1(e):
        e.tensor_scalar_mul(mean, st1, 1.0 / D)
        e.tensor_mul(slot_t, mean, mean)
        e.scalar_tensor_tensor(slot_s, st2, 1.0 / D, slot_t,
                               ALU.mult, ALU.subtract)
        return e.tensor_scalar_add(slot_s, slot_s, EPS)
    p.on("vector", rows1,
         waits=[(s_pe, hists["stats"][k]), hist(hists["bcast"], k - 1, s_pe)],
         inc=(s_dve, 1))
    hists["rows"].append(s_dve.n)

    def rows2(e):
        return e.activation(slot_t, slot_s, AF.Sqrt)
    p.on("scalar", rows2, waits=[(s_dve, hists["rows"][k])], inc=(s_act, 1))
    act_sd = s_act.n

    def rows3(e):
        e.reciprocal(slot_s, slot_t)
        return e.tensor_mul(slot_t, mean, slot_s)
    p.on("vector", rows3, waits=[(s_act, act_sd)], inc=(s_dve, 1))
    hists["ab"].append(s_dve.n)
    rowA, rowB = slot_s, slot_t

    psa, psb = bufs["ps_ab"](par)
    ab_sb = bufs.get("ab_sbuf")

    def bcast(e):
        e.matmul(psa, bufs["ones_row_f"][:], rowA, start=True, stop=True)
        return e.matmul(psb, bufs["ones_row_f"][:], rowB, start=True, stop=True)
    guard = (hist(hists["abe"], k - 1, s_dve) if ab_sb is not None
             else hist(hists["norm"], k - 2, s_dve))
    p.on("tensor", bcast,
         waits=[(s_dve, hists["ab"][k]), guard, extra],
         inc=(s_pe, 1))
    hists["bcast"].append(s_pe.n)

    tmp = bufs["ntmp"]
    if ab_sb is not None:
        a_sb, b_sb = ab_sb

        def abev(e):
            e.tensor_copy(a_sb[:], psa)
            return e.tensor_copy(b_sb[:], psb)
        p.on("vector", abev,
             waits=[(s_pe, hists["bcast"][k]), hist(hists["norm"], k - 1, s_dve)],
             inc=(s_dve, 1))
        hists["abe"].append(s_dve.n)
        src_a, src_b = a_sb[:], b_sb[:]
        nwaits = [(s_dve, hists["abe"][k])]
    else:
        src_a, src_b = psa, psb
        nwaits = [(s_pe, hists["bcast"][k])]

    def norm(e):
        for c in range(CD):
            e.tensor_mul(tmp[:, 0:tw], x_chunks[c], src_a)
            i = e.tensor_sub(xn_chunks[c], tmp[:, 0:tw], src_b)
        return i
    p.on("vector", norm, waits=nwaits, inc=(s_dve, 1))
    hists["norm"].append(s_dve.n)


def new_ln_hists():
    return {n: [] for n in ("cp", "sq", "stats", "rows", "ab", "bcast", "norm",
                            "abe")}



def build_progC():
    nc = bass.Bass("TRN2", target_bir_lowering=False, debug=False)
    NB = B // NCORES

    xin = nc.dram_tensor("xin", [NB, D, S], F32, kind="ExternalInput").ap()
    w1t = nc.dram_tensor("w1t", [128, 8 * D], F32, kind="ExternalInput").ap()
    w2t = nc.dram_tensor("w2t", [128, 8], F32, kind="ExternalInput").ap()
    wct = nc.dram_tensor("wct", [128, 16 * D], F32, kind="ExternalInput").ap()
    b1c = nc.dram_tensor("b1c", [128, 8], F32, kind="ExternalInput").ap()
    bcc = nc.dram_tensor("bcc", [128, 8], F32, kind="ExternalInput").ap()
    xout = nc.dram_tensor("xout", [NB, D, S], F32, kind="ExternalOutput").ap()

    NT = S // T

    with ExitStack() as ctx:
        E = ctx.enter_context
        s_w1 = E(nc.sbuf_tensor("s_w1", [128, 8 * D], BF16))
        s_w2 = E(nc.sbuf_tensor("s_w2", [128, 8], BF16))
        s_wc = E(nc.sbuf_tensor("s_wc", [128, 16 * D], BF16))
        s_b1 = E(nc.sbuf_tensor("s_b1", [128, 8], F32))
        s_bc = E(nc.sbuf_tensor("s_bc", [128, 8], F32))
        ones_col_bf = E(nc.sbuf_tensor("ones_col_bf", [128, 1], BF16))
        ones_row_f = E(nc.sbuf_tensor("ones_row_f", [1, 128], F32))
        ones_col_f = E(nc.sbuf_tensor("ones_col_f", [128, 1], F32))
        s_x = [E(nc.sbuf_tensor("s_x0", [128, CD * T], F32))]
        s_xbf = [E(nc.sbuf_tensor("s_xbf0", [128, CD * T], BF16))]
        s_xsq = [E(nc.sbuf_tensor("s_xsq0", [128, CD * T], BF16))]
        s_xn = E(nc.sbuf_tensor("s_xn", [128, CD * S], BF16))
        s_h = [E(nc.sbuf_tensor("s_h0", [128, CD * T], BF16))]
        s_ctx = E(nc.sbuf_tensor("s_ctx", [128, CD * S], BF16))
        s_ex = E(nc.sbuf_tensor("s_ex", [128, S], BF16))
        s_num = E(nc.sbuf_tensor("s_num", [128, S], F32))
        s_t2 = [E(nc.sbuf_tensor(f"s_t2{i}", [128, T], F32)) for i in range(2)]
        s_xo = [E(nc.sbuf_tensor(f"s_xo{i}", [128, T], F32)) for i in range(2)]
        r_rows = E(nc.sbuf_tensor("r_rows", [1, 3 * T], F32))
        r_sc = E(nc.sbuf_tensor("r_sc", [1, S], F32))
        r_e = E(nc.sbuf_tensor("r_e", [1, S], F32))
        r_den = E(nc.sbuf_tensor("r_den", [1, S], F32))
        r_m = E(nc.sbuf_tensor("r_m", [1, 2], F32))

        psA = E(nc.psum_tensor("psA", [128, 2048], F32))
        psB = E(nc.psum_tensor("psB", [128, 2048], F32))

        s_lw = Sem(ctx, nc, "s_lw")
        s_lx = Sem(ctx, nc, "s_lx")
        s_pe = Sem(ctx, nc, "s_pe")
        s_act = Sem(ctx, nc, "s_act")
        s_dve = Sem(ctx, nc, "s_dve")
        s_st = Sem(ctx, nc, "s_st")
        s_sta = Sem(ctx, nc, "s_sta")
        s_stb = Sem(ctx, nc, "s_stb")
        sems = {"pe": s_pe, "act": s_act, "dve": s_dve}

        p = Prog(nc)

        p.on("vector", lambda e: e.memset(ones_col_bf[:], 1.0))
        p.on("vector", lambda e: e.memset(ones_row_f[:], 1.0))
        p.on("vector", lambda e: e.memset(ones_col_f[:], 1.0), inc=(s_dve, 1))
        dve_const = s_dve.n
        p.dma("gpsimd", s_w1[:], w1t[:], sem=s_lw)
        p.dma("gpsimd", s_w2[:], w2t[:], sem=s_lw)
        p.dma("gpsimd", s_wc[:], wct[:], sem=s_lw)
        p.dma("sync", s_b1[:], b1c[:], sem=s_lx)
        p.dma("sync", s_bc[:], bcc[:], sem=s_lx)
        w_ready = s_lw.n
        b_ready = s_lx.n

        ln_bufs = {
            "xbf": s_xbf, "xsq": s_xsq, "rows": r_rows,
            "ones_col_bf": ones_col_bf, "ones_row_f": ones_row_f,
            "ntmp": s_num,
            "ps_stat": lambda par: (psA[0:1, par * 512:par * 512 + T],
                                    psA[0:1, 1024 + par * 512:1024 + par * 512 + T]),
            "ps_ab": lambda par: (psB[:, par * 1024:par * 1024 + T],
                                  psB[:, par * 1024 + T:par * 1024 + 2 * T]),
        }
        lh = new_ln_hists()

        h_tanh = []
        y_tanh = []
        y_res = []
        sc_evac = []
        store_hist = []
        batch_end = []

        for b in range(NB):
            xn_chunks_all = []
            extra_ps = (s_dve, prev_ctx) if b > 0 else None
            for t in range(NT):
                kln = len(lh["norm"])
                par = kln % 2
                t0 = t * T
                if t == 0:
                    lw = [hist(y_res, len(y_res) - 1, s_dve)]
                else:
                    lw = [hist(lh["norm"], kln - 1, s_dve)]
                p.dma("sync", s_x[0][:].rearrange("p (c s) -> p c s", s=T),
                      xtile_ap(xin, b, t0, T), waits=lw, sem=s_lx)
                x_ready = (s_lx, s_lx.n)
                x_chunks = [s_x[0][:, c * T:(c + 1) * T] for c in range(CD)]
                xn_chunks = [s_xn[:, c * S + t0:c * S + t0 + T] for c in range(CD)]
                emit_ln_tile(p, sems, ln_bufs, x_chunks, xn_chunks, x_ready, lh,
                             extra=extra_ps)
            ln_done = lh["norm"][-1]

            for t in range(NT):
                par = 0
                t0 = t * T
                for oc in range(CD):
                    kh = len(h_tanh)
                    pof = (kh % 2) * T

                    def hmm(oc=oc, pof=pof, t0=t0):
                        def f(e):
                            for c in range(CD):
                                i = e.matmul(psA[:, pof:pof + T],
                                             s_w1[:, c * D + oc * 128:c * D + (oc + 1) * 128],
                                             s_xn[:, c * S + t0:c * S + t0 + T],
                                             start=(c == 0), stop=(c == CD - 1))
                            return i
                        return f
                    p.on("tensor", hmm(),
                         waits=[(s_lw, w_ready), (s_lx, b_ready), (s_dve, ln_done),
                                hist(h_tanh, kh - 2, s_act)],
                         inc=(s_pe, 1))
                    pe_h = s_pe.n

                    def tanh(oc=oc, pof=pof, par=par):
                        def f(e):
                            return e.activation(s_h[par][:, oc * T:(oc + 1) * T],
                                                psA[:, pof:pof + T], AF.Tanh,
                                                bias=s_b1[:, oc:oc + 1])
                        return f
                    p.on("scalar", tanh(),
                         waits=[(s_pe, pe_h),
                                hist(sc_evac, len(sc_evac) - 1, s_dve)],
                         inc=(s_act, 1))
                    h_tanh.append(s_act.n)

                def sc(par=par):
                    def f(e):
                        for c in range(CD):
                            i = e.matmul(psA[0:1, 1024:1024 + T], s_w2[:, c:c + 1],
                                         s_h[par][:, c * T:(c + 1) * T],
                                         start=(c == 0), stop=(c == CD - 1))
                        return i
                    return f
                p.on("tensor", sc(),
                     waits=[(s_act, h_tanh[-1]),
                            hist(sc_evac, len(sc_evac) - 1, s_dve)],
                     inc=(s_pe, 1))
                pe_sc = s_pe.n

                def scev(t0=t0):
                    def f(e):
                        return e.tensor_copy(r_sc[0:1, t0:t0 + T], psA[0:1, 1024:1024 + T])
                    return f
                p.on("vector", scev(), waits=[(s_pe, pe_sc)], inc=(s_dve, 1))
                sc_evac.append(s_dve.n)

            def mrow(e):
                e.reduce_max(r_m[0:1, 0:1], r_sc[0:1, :], axis=AX.X)
                return e.tensor_scalar_mul(r_m[0:1, 1:2], r_m[0:1, 0:1], -1.0)
            p.on("vector", mrow, waits=[(s_dve, sc_evac[-1])], inc=(s_dve, 1))
            dve_m = s_dve.n

            p.on("scalar",
                 lambda e: e.activation(r_e[0:1, :], r_sc[0:1, :], AF.Exp, bias=r_m[0:1, 1:2]),
                 waits=[(s_dve, dve_m)], inc=(s_act, 1))
            act_e = s_act.n

            def denrow(e):
                e.tensor_tensor_scan(r_den[0:1, :],
                                     ones_col_f[0:1, 0:1].broadcast_to((1, S)),
                                     r_e[0:1, :], 0.0, ALU.mult, ALU.add)
                return e.reciprocal(r_sc[0:1, :], r_den[0:1, :])
            p.on("vector", denrow, waits=[(s_act, act_e), (s_dve, dve_const)],
                 inc=(s_dve, 1))
            dve_rden = s_dve.n

            def ebc(e):
                for j in range(4):
                    e.matmul(psA[:, j * T:(j + 1) * T], ones_row_f[:],
                             r_e[0:1, j * T:(j + 1) * T], start=True, stop=True)
                for j in range(4):
                    i = e.matmul(psB[:, j * T:(j + 1) * T], ones_row_f[:],
                                 r_sc[0:1, j * T:(j + 1) * T], start=True, stop=True)
                return i
            p.on("tensor", ebc, waits=[(s_dve, dve_rden)], inc=(s_pe, 1))
            pe_ebc = s_pe.n

            for c in range(CD):
                def exf(c=c):
                    def f(e):
                        e.tensor_mul(s_ex[:], s_xn[:, c * S:(c + 1) * S], psA[:, 0:S])
                        e.tensor_tensor_scan(s_num[:],
                                             ones_col_f[:].broadcast_to((128, S)),
                                             s_ex[:], 0.0, ALU.mult, ALU.add)
                        return e.tensor_mul(s_ctx[:, c * S:(c + 1) * S], s_num[:],
                                            psB[:, 0:S])
                    return f
                p.on("vector", exf(), waits=[(s_pe, pe_ebc)], inc=(s_dve, 1))
            dve_ctx = s_dve.n
            prev_ctx = dve_ctx

            for t in range(NT):
                par = t % 2
                t0 = t * T
                lw = [(s_dve, dve_ctx)]
                if t >= 1:
                    lw.append(hist(y_res, len(y_res) - 1, s_dve))
                p.dma("sync", s_x[0][:].rearrange("p (c s) -> p c s", s=T),
                      xtile_ap(xin, b, t0, T), waits=lw, sem=s_lx)
                x_rdy = s_lx.n

                for oc in range(CD):
                    ky = len(y_tanh)
                    pof = (ky % 2) * T

                    def ymm(oc=oc, pof=pof, t0=t0):
                        def f(e):
                            for kc in range(16):
                                rhs = (s_ctx[:, kc * S + t0:kc * S + t0 + T] if kc < CD
                                       else s_xn[:, (kc - CD) * S + t0:(kc - CD) * S + t0 + T])
                                i = e.matmul(psA[:, pof:pof + T],
                                             s_wc[:, kc * D + oc * 128:kc * D + (oc + 1) * 128],
                                             rhs, start=(kc == 0), stop=(kc == 15))
                            return i
                        return f
                    p.on("tensor", ymm(),
                         waits=[(s_dve, dve_ctx), hist(y_tanh, ky - 2, s_act)],
                         inc=(s_pe, 1))
                    pe_y = s_pe.n

                    def t2f(pof=pof, oc=oc, tb=ky % 2):
                        def f(e):
                            return e.activation(s_t2[tb][:], psA[:, pof:pof + T],
                                                AF.Tanh, bias=s_bc[:, oc:oc + 1])
                        return f
                    p.on("scalar", t2f(),
                         waits=[(s_pe, pe_y), hist(y_res, ky - 2, s_dve)],
                         inc=(s_act, 1))
                    y_tanh.append(s_act.n)

                    def resf(oc=oc, tb=ky % 2, ob=ky % 2):
                        def f(e):
                            return e.tensor_add(s_xo[ob][:],
                                                s_x[0][:, oc * T:(oc + 1) * T],
                                                s_t2[tb][:])
                        return f
                    stp = s_sta if ky % 2 == 0 else s_stb
                    wr = [(s_act, y_tanh[-1]), (s_lx, x_rdy), (stp, stp.n)]
                    p.on("vector", resf(), waits=wr, inc=(s_dve, 1))
                    y_res.append(s_dve.n)

                    p.dma("sync",
                          xout[b].rearrange("(c p) s -> c p s", p=128)[oc, :, t0:t0 + T],
                          s_xo[ky % 2][:],
                          waits=[(s_dve, y_res[-1])], sem=stp)
                    store_hist.append((stp, stp.n))
            batch_end.append(y_res[-1])

        p.on("sync", lambda e: e.nop(),
             waits=[(s_sta, s_sta.n), (s_stb, s_stb.n)])

        with nc.Block() as block:
            p.replay(block)

    return nc


def prep_progC_weights(W1, b1, w2, b2, Wc, bc):
    del b2
    return {
        "w1t": chunk_major(np.ascontiguousarray(W1.T)),
        "w2t": chunk_major(np.ascontiguousarray(w2.T)),
        "wct": chunk_major(np.ascontiguousarray(Wc.T)),
        "b1c": col8(b1),
        "bcc": col8(bc),
    }


def run_progC(x_fm, weights, nc=None):
    if nc is None:
        nc = build_progC()
    nb = B // NCORES
    in_maps = []
    for c in range(NCORES):
        m = {"xin": np.ascontiguousarray(x_fm[c * nb:(c + 1) * nb])}
        m.update(weights)
        in_maps.append(m)
    res = run_bass_kernel_spmd(nc, in_maps, core_ids=list(range(NCORES)))
    return np.concatenate([r["xout"] for r in res.results], axis=0)



SP = 256
NSTEP = 32


def build_progB():
    nc = bass.Bass("TRN2", target_bir_lowering=False, debug=False)

    xin = nc.dram_tensor("xin", [NSTEP, D, SP], F32, kind="ExternalInput").ap()
    wnames = ["wf", "wi", "wc", "wo", "uf", "ui", "uc", "uo"]
    wd = {n: nc.dram_tensor(n + "t", [128, 8 * D], F32, kind="ExternalInput").ap()
          for n in wnames}
    bd = {n: nc.dram_tensor("b" + n, [128, 8], F32, kind="ExternalInput").ap()
          for n in ["f", "i", "c", "o"]}
    incol = nc.dram_tensor("incol", [128, 8], F32, kind="ExternalInput").ap()
    xout = nc.dram_tensor("xout", [NSTEP, D, SP], F32, kind="ExternalOutput").ap()
    xnscr = nc.dram_tensor("xnscr", [NSTEP, D, SP], BF16, kind="Internal").ap()

    tw = SP

    with ExitStack() as ctx:
        E = ctx.enter_context
        sw = {n: E(nc.sbuf_tensor("s_" + n, [128, 8 * D], BF16)) for n in wnames}
        sb = {n: E(nc.sbuf_tensor("s_b" + n, [128, 8], F32)) for n in ["f", "i", "c", "o"]}
        s_incol = E(nc.sbuf_tensor("s_incol", [128, 8], F32))
        ones_col_bf = E(nc.sbuf_tensor("ones_col_bf", [128, 1], BF16))
        ones_row_f = E(nc.sbuf_tensor("ones_row_f", [1, 128], F32))
        s_x = E(nc.sbuf_tensor("s_x0", [128, CD * tw], F32))
        s_xbf = [E(nc.sbuf_tensor("s_xbf0", [128, CD * tw], BF16))]
        s_xsq = [E(nc.sbuf_tensor("s_xsq0", [128, CD * tw], BF16))]
        s_xnt = E(nc.sbuf_tensor("s_xnt", [128, CD * tw], BF16))
        s_xn = [E(nc.sbuf_tensor(f"s_xn{i}", [128, CD * tw], BF16)) for i in range(2)]
        s_c = E(nc.sbuf_tensor("s_c", [128, CD * tw], F32))
        s_cbf = E(nc.sbuf_tensor("s_cbf", [128, CD * tw], BF16))
        s_g = E(nc.sbuf_tensor("s_g", [128, 3 * CD * tw], F32))
        s_gt1 = E(nc.sbuf_tensor("s_gt1", [128, tw], F32))
        s_gt2 = E(nc.sbuf_tensor("s_gt2", [128, tw], F32))
        s_cu = s_g
        s_t2 = [E(nc.sbuf_tensor(f"s_t2{i}", [128, tw], F32)) for i in range(2)]
        s_tm = [s_gt1, s_gt2]
        s_xo = [E(nc.sbuf_tensor(f"s_xo{i}", [128, tw], F32)) for i in range(2)]
        s_ntmp = s_gt1
        r_rows = E(nc.sbuf_tensor("r_rows", [1, 3 * tw], F32))

        psA = E(nc.psum_tensor("psA", [128, 2048], F32))
        psB = E(nc.psum_tensor("psB", [128, 2048], F32))

        s_lw = Sem(ctx, nc, "s_lw")
        s_lx = Sem(ctx, nc, "s_lx")
        s_ln = Sem(ctx, nc, "s_ln")
        s_pe = Sem(ctx, nc, "s_pe")
        s_act = Sem(ctx, nc, "s_act")
        s_dve = Sem(ctx, nc, "s_dve")
        s_st = Sem(ctx, nc, "s_st")
        s_sta = Sem(ctx, nc, "s_sta")
        s_stb = Sem(ctx, nc, "s_stb")
        sems = {"pe": s_pe, "act": s_act, "dve": s_dve}

        p = Prog(nc)

        p.on("vector", lambda e: e.memset(ones_col_bf[:], 1.0))
        p.on("vector", lambda e: e.memset(ones_row_f[:], 1.0), inc=(s_dve, 1))
        for n in wnames:
            p.dma("gpsimd", sw[n][:], wd[n], sem=s_lw)
        for n in ["f", "i", "c", "o"]:
            p.dma("sync", sb[n][:], bd[n], sem=s_lx)
        p.dma("sync", s_incol[:], incol[:], sem=s_lx)
        w_ready = s_lw.n
        b_ready = s_lx.n

        ln_bufs = {
            "xbf": s_xbf, "xsq": s_xsq, "rows": r_rows,
            "ones_col_bf": ones_col_bf, "ones_row_f": ones_row_f,
            "ntmp": s_ntmp,
            "ps_stat": lambda par: (psA[0:1, par * 512:par * 512 + tw],
                                    psA[0:1, 1024 + par * 512:1024 + par * 512 + tw]),
            "ps_ab": lambda par: (psB[:, par * 1024:par * 1024 + tw],
                                  psB[:, par * 1024 + tw:par * 1024 + 2 * tw]),
        }
        lh = new_ln_hists()

        ln_store = []
        for t in range(NSTEP):
            kln = len(lh["norm"])
            lw = [hist(lh["norm"], kln - 1, s_dve)]
            if t >= 1:
                lw.append((s_st, ln_store[-1]))
            p.dma("sync", s_x[:].rearrange("p (c s) -> p c s", s=tw),
                  xtile_ap(xin, t, 0, tw), waits=lw, sem=s_lx)
            x_ready = (s_lx, s_lx.n)
            x_chunks = [s_x[:, c * tw:(c + 1) * tw] for c in range(CD)]
            xn_chunks = [s_xnt[:, c * tw:(c + 1) * tw] for c in range(CD)]
            if t > 0:
                pass
            emit_ln_tile(p, sems, ln_bufs, x_chunks, xn_chunks, x_ready, lh, tw=tw)
            p.dma("sync", xtile_ap(xnscr, t, 0, tw),
                  s_xnt[:].rearrange("p (c s) -> p c s", s=tw),
                  waits=[(s_dve, lh["norm"][-1])], sem=s_st)
            ln_store.append(s_st.n)

        def cinit(e):
            for oc in range(CD):
                e.tensor_copy(s_c[:, oc * tw:(oc + 1) * tw],
                              s_incol[:, oc:oc + 1].broadcast_to((128, tw)))
            for oc in range(CD):
                i = e.tensor_copy(s_cbf[:, oc * tw:(oc + 1) * tw],
                                  s_c[:, oc * tw:(oc + 1) * tw])
            return i
        p.on("vector", cinit, waits=[(s_lw, w_ready), (s_lx, b_ready)],
             inc=(s_dve, 1))
        dve_c = s_dve.n

        GN = ["f", "i", "c"]
        gate_evac = []
        xn_load = []
        for t in range(NSTEP):
            par = t % 2
            lw = [(s_st, ln_store[-1])]
            if t >= 2:
                lw.append(hist(xn_load, t - 2, s_pe))
            p.dma("sync", s_xn[par][:].rearrange("p (c s) -> p c s", s=tw),
                  xtile_ap(xnscr, t, 0, tw), waits=lw, sem=s_ln)
            xn_rdy = s_ln.n

            pe_last_group = None
            for oc in range(CD):
                for gi, g in enumerate(GN):
                    kg = len(gate_evac)
                    pof = (kg % 4) * 512

                    def gmm(g=g, oc=oc, pof=pof, par=par):
                        def f(e):
                            for c in range(CD):
                                e.matmul(psA[:, pof:pof + tw],
                                         sw["w" + g][:, c * D + oc * 128:c * D + (oc + 1) * 128],
                                         s_xn[par][:, c * tw:(c + 1) * tw],
                                         start=(c == 0), stop=False)
                            for c in range(CD):
                                i = e.matmul(psA[:, pof:pof + tw],
                                             sw["u" + g][:, c * D + oc * 128:c * D + (oc + 1) * 128],
                                             s_cbf[:, c * tw:(c + 1) * tw],
                                             start=False, stop=(c == CD - 1))
                            return i
                        return f
                    waits = [(s_ln, xn_rdy), (s_dve, dve_c),
                             hist(gate_evac, kg - 4, s_act)]
                    if t == 0 and oc == 0 and gi == 0:
                        waits.append((s_dve, lh["norm"][-1]))
                    p.on("tensor", gmm(), waits=waits, inc=(s_pe, 1))
                    pe_g = s_pe.n

                    def gact(g=g, gi=gi, oc=oc, pof=pof):
                        fn = AF.Tanh if g == "c" else AF.Sigmoid
                        off = (gi * CD + oc) * tw

                        def f(e):
                            return e.activation(s_g[:, off:off + tw],
                                                psA[:, pof:pof + tw], fn,
                                                bias=sb[g][:, oc:oc + 1])
                        return f
                    p.on("scalar", gact(),
                         waits=[(s_pe, pe_g), (s_dve, dve_c)],
                         inc=(s_act, 1))
                    gate_evac.append(s_act.n)
                pe_last_group = s_pe.n
            xn_load.append(pe_last_group)

            def cupd(e):
                for oc in range(CD):
                    fo = (0 * CD + oc) * tw
                    io = (1 * CD + oc) * tw
                    go = (2 * CD + oc) * tw
                    e.tensor_mul(s_gt1[:], s_g[:, fo:fo + tw],
                                 s_c[:, oc * tw:(oc + 1) * tw])
                    e.tensor_mul(s_gt2[:], s_g[:, io:io + tw], s_g[:, go:go + tw])
                    e.tensor_add(s_c[:, oc * tw:(oc + 1) * tw], s_gt1[:], s_gt2[:])
                    i = e.tensor_copy(s_cbf[:, oc * tw:(oc + 1) * tw],
                                      s_c[:, oc * tw:(oc + 1) * tw])
                return i
            p.on("vector", cupd, waits=[(s_act, gate_evac[-1])], inc=(s_dve, 1))
            dve_c = s_dve.n

        cu_evac = []
        for oc in range(CD):
            pof = (oc % 2) * 512

            def cumm(oc=oc, pof=pof):
                def f(e):
                    for c in range(CD):
                        i = e.matmul(psB[:, pof:pof + tw],
                                     sw["uo"][:, c * D + oc * 128:c * D + (oc + 1) * 128],
                                     s_cbf[:, c * tw:(c + 1) * tw],
                                     start=(c == 0), stop=(c == CD - 1))
                    return i
                return f
            p.on("tensor", cumm(),
                 waits=[(s_dve, dve_c), hist(cu_evac, oc - 2, s_dve)],
                 inc=(s_pe, 1))
            pe_cu = s_pe.n

            def cuev(oc=oc, pof=pof):
                def f(e):
                    return e.tensor_copy(s_cu[:, oc * tw:(oc + 1) * tw],
                                         psB[:, pof:pof + tw])
                return f
            p.on("vector", cuev(), waits=[(s_pe, pe_cu)], inc=(s_dve, 1))
            cu_evac.append(s_dve.n)
        dve_cu = cu_evac[-1]

        y_tanh = []
        y_res = []
        store_hist = []
        for t in range(NSTEP):
            par = t % 2
            lw = [(s_dve, dve_cu)]
            if t >= 2:
                lw.append(hist(xn_load2, t - 2, s_pe))
            p.dma("sync", s_xn[par][:].rearrange("p (c s) -> p c s", s=tw),
                  xtile_ap(xnscr, t, 0, tw), waits=lw, sem=s_ln)
            xn_rdy = s_ln.n
            if t == 0:
                xn_load2 = []
            lwx = [(s_dve, dve_cu)] if t == 0 else [(s_dve, y_res[-1])]
            p.dma("sync", s_x[:].rearrange("p (c s) -> p c s", s=tw),
                  xtile_ap(xin, t, 0, tw), waits=lwx, sem=s_lx)
            x_rdy = s_lx.n

            pe_last = None
            for oc in range(CD):
                ky = len(y_tanh)
                pof = (ky % 2) * 512

                def ymm(oc=oc, pof=pof, par=par):
                    def f(e):
                        for c in range(CD):
                            i = e.matmul(psA[:, pof:pof + tw],
                                         sw["wo"][:, c * D + oc * 128:c * D + (oc + 1) * 128],
                                         s_xn[par][:, c * tw:(c + 1) * tw],
                                         start=(c == 0), stop=(c == CD - 1))
                        return i
                    return f
                w0 = [(s_ln, xn_rdy), hist(y_tanh, ky - 2, s_act)]
                if t == 0 and oc == 0:
                    w0.append((s_act, gate_evac[-1]))
                p.on("tensor", ymm(), waits=w0, inc=(s_pe, 1))
                pe_y = s_pe.n
                pe_last = pe_y

                def pre(oc=oc, pof=pof, tb=ky % 2):
                    def f(e):
                        return e.scalar_tensor_tensor(
                            s_tm[tb][:], psA[:, pof:pof + tw], sb["o"][:, oc:oc + 1],
                            s_cu[:, oc * tw:(oc + 1) * tw], ALU.add, ALU.add)
                    return f
                wpre = [(s_pe, pe_y), hist(y_tanh, ky - 2, s_act)]
                p.on("vector", pre(), waits=wpre, inc=(s_dve, 1))
                dve_pre = s_dve.n

                def t2f(tb=ky % 2):
                    def f(e):
                        return e.activation(s_t2[tb][:], s_tm[tb][:], AF.Tanh)
                    return f
                wt2 = [(s_dve, dve_pre), hist(y_res, ky - 2, s_dve)]
                p.on("scalar", t2f(), waits=wt2, inc=(s_act, 1))
                y_tanh.append(s_act.n)

                def resf(oc=oc, tb=ky % 2):
                    def f(e):
                        return e.scalar_tensor_tensor(
                            s_xo[tb][:], s_t2[tb][:], 0.5,
                            s_x[:, oc * tw:(oc + 1) * tw], ALU.mult, ALU.add)
                    return f
                stp = s_sta if ky % 2 == 0 else s_stb
                wr = [(s_act, y_tanh[-1]), (s_lx, x_rdy), (stp, stp.n)]
                p.on("vector", resf(), waits=wr, inc=(s_dve, 1))
                y_res.append(s_dve.n)

                p.dma("sync",
                      xout[t].rearrange("(c p) s -> c p s", p=128)[oc, :, 0:tw],
                      s_xo[ky % 2][:], waits=[(s_dve, y_res[-1])], sem=stp)
                store_hist.append((stp, stp.n))
            xn_load2.append(pe_last)

        p.on("sync", lambda e: e.nop(),
             waits=[(s_sta, s_sta.n), (s_stb, s_stb.n), (s_st, s_st.n)])

        with nc.Block() as block:
            p.replay(block)

    return nc


def prep_progB_weights(wf, uf, bf, wi, ui, bi, wc, uc, bc, wo, uo, bo, in_c):
    d = {}
    for n, w in [("wf", wf), ("wi", wi), ("wc", wc), ("wo", wo),
                 ("uf", uf), ("ui", ui), ("uc", uc), ("uo", uo)]:
        d[n + "t"] = chunk_major(np.ascontiguousarray(w.T))
    for n, v in [("f", bf), ("i", bi), ("c", bc), ("o", bo)]:
        d["b" + n] = col8(v)
    d["incol"] = col8(in_c)
    return d


def run_progB(x_fm, weights, nc=None):
    if nc is None:
        nc = build_progB()
    in_maps = []
    for c in range(NCORES):
        m = {"xin": np.ascontiguousarray(x_fm[:, :, c * SP:(c + 1) * SP])}
        m.update(weights)
        in_maps.append(m)
    res = run_bass_kernel_spmd(nc, in_maps, core_ids=list(range(NCORES)))
    out = np.empty((NSTEP, D, S), np.float32)
    for c in range(NCORES):
        out[:, :, c * SP:(c + 1) * SP] = res.results[c]["xout"]
    return out



def _emit_ffn_sweep(p, sems, env, src, dst):
    s_lw, s_lx, s_pe, s_act, s_dve, s_st = (env[k] for k in
                                            ("s_lw", "s_lx", "s_pe", "s_act",
                                             "s_dve", "s_st"))
    sems3 = {"pe": s_pe, "act": s_act, "dve": s_dve}
    NB = B // NCORES
    lh = env["mk_lnh"]()
    zw_hist = env["zw_hist"]
    y_tanh = env["y_tanh"]
    y_res = env["y_res"]
    st_hist = env["st_hist"]
    s_x, s_xbf = env["s_x"], env["s_xbf"]
    psP1, psP2, psC = env["psP1"], env["psP2"], env["psC"]

    for b in range(NB):
        for t in range(4):
            t0 = t * T
            kf = len(zw_hist)
            lw = [hist(y_res, len(y_res) - 1, s_dve), env["entry"]()]
            p.dma("sync", s_x[:].rearrange("p (c s) -> p c s", s=T),
                  xtile_ap(src, b, t0, T), waits=lw, sem=s_lx)
            x_ready = (s_lx, s_lx.n)
            x_chunks = [s_x[:, c * T:(c + 1) * T] for c in range(CD)]
            xn_chunks = [s_xbf[0][:, c * T:(c + 1) * T] for c in range(CD)]
            cpx = hist(env["ffn_ymm"], len(env["ffn_ymm"]) - 1, s_pe)
            emit_ln_tile(p, sems3, env["ln_bufs"], x_chunks, xn_chunks,
                         x_ready, lh, cp_extra=cpx, extra=env["entry"]())
            norm_k = lh["norm"][-1]

            if env.get("no_moe"):
                zw_hist.append(norm_k)
                dve_zw = norm_k
            else:
                def lgz(b_=b, t0=t0):
                    def f(e):
                        for q in range(4):
                            for c in range(CD):
                                e.matmul(psP2[:, 1024 + q * 16:1024 + q * 16 + 10],
                                         s_xbf[0][:, c * T + q * 128:c * T + (q + 1) * 128],
                                         env["s_gw"][:, c * 10:(c + 1) * 10],
                                         start=(c == 0), stop=(c == CD - 1))
                        for c in range(CD):
                            i = e.matmul(psC[:, 0:T], env["s_la"][:, c * 128:(c + 1) * 128],
                                         s_xbf[0][:, c * T:(c + 1) * T],
                                         start=(c == 0), stop=(c == CD - 1))
                        return i
                    return f
                p.on("tensor", lgz(),
                     waits=[(s_dve, norm_k), hist(zw_hist, kf - 1, s_dve),
                            (s_lw, env["w_ready"])],
                     inc=(s_pe, 1))
                pe_lgz = s_pe.n

                s_lg, s_gi, s_gs, s_ewf = (env[k] for k in
                                           ("s_lg", "s_gi", "s_gs", "s_ewf"))

                def lgev(e):
                    return e.tensor_copy(
                        s_lg[:].rearrange("p (q v) -> p q v", v=10),
                        psP2[:, 1024:1024 + 4 * 16].rearrange("p (q v) -> p q v", v=16)[:, :, 0:10])
                p.on("vector", lgev, waits=[(s_pe, pe_lgz)], inc=(s_dve, 1))
                dve_lgev = s_dve.n

                s_gl = env["s_gl"]
                lgv = s_lg[:].rearrange("p (q v) -> p q v", v=10)
                i1v = s_gi[:, 0:40].rearrange("p (q v) -> p q v", v=10)
                i2v = s_gi[:, 40:80].rearrange("p (q v) -> p q v", v=10)
                l2v = s_gl[:].rearrange("p (q v) -> p q v", v=10)
                gsb = lambda j: s_gs[:, 4 * j:4 * (j + 1)]
                gsv = lambda j: s_gs[:, 4 * j:4 * (j + 1)].rearrange(
                    "p (q o) -> p q o", o=1)

                p.chain("vector", [
                    lambda e: e.reduce_max(gsb(0), lgv, axis=AX.X),
                    lambda e: e.tensor_tensor(
                        i1v, lgv, gsv(0).broadcast_to((128, 4, 10)), ALU.is_ge),
                    lambda e: e.scalar_tensor_tensor(l2v, i1v, -1e30, lgv,
                                                     ALU.mult, ALU.add),
                    lambda e: e.reduce_max(gsb(1), l2v, axis=AX.X),
                    lambda e: e.tensor_tensor(
                        i2v, l2v, gsv(1).broadcast_to((128, 4, 10)), ALU.is_ge),
                    lambda e: e.tensor_sub(gsb(2), gsb(1), gsb(0)),
                ], s_dve, waits=[(s_dve, dve_lgev), env.get("prev_ge"),
                                 env.get("dbg_wait")])
                dve_g1 = s_dve.n

                p.on("scalar", lambda e: e.activation(gsb(3), gsb(2), AF.Exp),
                     waits=[(s_dve, dve_g1)], inc=(s_act, 1))
                act_ge = s_act.n
                env["prev_ge"] = (s_act, act_ge)

                p.chain("vector", [
                    lambda e: e.reduce_sum(gsb(4), i1v[:, :, 0:8], axis=AX.X),
                    lambda e: e.reduce_sum(gsb(5), i2v[:, :, 0:8], axis=AX.X),
                    lambda e: e.tensor_mul(gsb(6), gsb(5), gsb(3)),
                    lambda e: e.tensor_add(gsb(7), gsb(6), gsb(4)),
                    lambda e: e.tensor_scalar_max(gsb(8), gsb(7), 1e-9),
                    lambda e: e.reciprocal(gsb(9), gsb(8)),
                    lambda e: e.tensor_mul(gsb(10), gsb(4), gsb(9)),
                    lambda e: e.tensor_mul(gsb(11), gsb(5), gsb(9)),
                    lambda e: e.tensor_mul(gsb(12), gsb(11), gsb(3)),
                    lambda e: e.tensor_tensor(
                        l2v[:, :, 0:8], i1v[:, :, 0:8],
                        gsv(10).broadcast_to((128, 4, 8)), ALU.mult),
                    lambda e: e.tensor_tensor(
                        i1v[:, :, 0:8], i2v[:, :, 0:8],
                        gsv(12).broadcast_to((128, 4, 8)), ALU.mult),
                    lambda e: e.tensor_tensor(i2v[:, :, 0:8], l2v[:, :, 0:8],
                                              i1v[:, :, 0:8], ALU.add),
                    lambda e: e.tensor_scalar_mul(
                        s_ewf[:].rearrange("p (q v) -> p q v", v=8),
                        i2v[:, :, 0:8], 2.0),
                ], s_dve, waits=[(s_act, act_ge)])
                dve_g2 = s_dve.n

                def ewtr(e):
                    for q in range(4):
                        i = e.transpose(psP2[0:8, 512 + q * 128:512 + (q + 1) * 128],
                                        s_ewf[:, q * 8:(q + 1) * 8], env["s_idf"][:])
                    return i
                p.on("tensor", ewtr,
                     waits=[(s_dve, dve_g2), hist(y_tanh, len(y_tanh) - 1, s_act)],
                     inc=(s_pe, 1))
                pe_tr = s_pe.n

                def ewev(e):
                    return e.tensor_copy(env["s_ewt"][:],
                                         psP2[0:8, 512:512 + 512])
                p.on("vector", ewev, waits=[(s_pe, pe_tr)], inc=(s_dve, 1))
                dve_ewt = s_dve.n

                def ewmm(e):
                    for q in range(4):
                        i = e.matmul(psP2[:, 1024 + q * 128:1024 + (q + 1) * 128],
                                     env["s_b16"][:], env["s_ewt"][:, q * 128:(q + 1) * 128],
                                     start=True, stop=True)
                    return i
                p.on("tensor", ewmm, waits=[(s_dve, dve_ewt), (s_dve, dve_lgev)],
                     inc=(s_pe, 1))
                pe_ew = s_pe.n

                def zf(e):
                    return e.activation(env["s_zf"][:], psC[:, 0:T], AF.Copy)
                p.on("scalar", zf, waits=[(s_pe, pe_lgz)], inc=(s_act, 1))
                act_zf = s_act.n

                def zwf(e):
                    return e.tensor_mul(env["s_zw"][:], env["s_zf"][:],
                                        psP2[:, 1024:1024 + T])
                p.on("vector", zwf, waits=[(s_act, act_zf), (s_pe, pe_ew)],
                     inc=(s_dve, 1))
                zw_hist.append(s_dve.n)
                dve_zw = s_dve.n

            dbg_tiles = env.get("dbg_tiles") or ()
            dbg = env.get("dbg")
            if dbg_tiles and (b, t) in dbg_tiles and env.get("dbg_live"):
                di = dbg_tiles.index((b, t))
                s_dbg = env["s_dbg"]

                def dbgf(di=di):
                    def f(e):
                        e.dma_start(out=dbg[di, :, 0:40], in_=env["s_lg"][:]).then_inc(s_dbg.h, 16)
                        e.dma_start(out=dbg[di, :, 40:104], in_=env["s_gs"][:]).then_inc(s_dbg.h, 16)
                        e.dma_start(out=dbg[di, :, 104:136], in_=env["s_ewf"][:]).then_inc(s_dbg.h, 16)
                        e.dma_start(out=dbg[di, :, 136:648], in_=env["s_zw"][:]).then_inc(s_dbg.h, 16)
                        e.dma_start(out=dbg[di, 0:8, 648:1160], in_=env["s_ewt"][:]).then_inc(s_dbg.h, 16)
                        i = e.dma_start(out=dbg[di, :, 1160:1184],
                                        in_=s_xbf[0][:, 0:24])
                        return i
                    return f
                s_dbg.n += 5 * 16
                p.on("gpsimd", dbgf(), waits=[(s_dve, dve_zw)], inc=(s_dbg, 16))
                env["dbg_wait"] = (s_dbg, s_dbg.n)

            for oc in range(CD):
                ky = len(y_tanh)
                pof = (ky % 2) * 512

                def ymm(oc=oc, pof=pof):
                    def f(e):
                        for c in range(CD):
                            i = e.matmul(psP2[:, pof:pof + T],
                                         env["s_bw"][:, c * D + oc * 128:c * D + (oc + 1) * 128],
                                         s_xbf[0][:, c * T:(c + 1) * T],
                                         start=(c == 0),
                                         stop=(bool(env.get("no_moe")) and c == CD - 1))
                        if env.get("no_moe"):
                            return i
                        return e.matmul(psP2[:, pof:pof + T],
                                        env["s_lb"][:, oc * 128:(oc + 1) * 128],
                                        env["s_zw"][:], start=False, stop=True)
                    return f
                p.on("tensor", ymm(),
                     waits=[(s_dve, dve_zw), hist(y_tanh, ky - 2, s_act),
                            (env["s_lw"], env["w_ready"]), env["b_ready"]],
                     inc=(s_pe, 1))
                env["ffn_ymm"].append(s_pe.n)
                pe_y = s_pe.n

                def t1f(oc=oc, pof=pof, tb=ky % 2):
                    def f(e):
                        return e.activation(env["s_t1"][tb][:], psP2[:, pof:pof + T],
                                            AF.Identity, bias=env["s_bb"][:, oc:oc + 1])
                    return f
                p.on("scalar", t1f(),
                     waits=[(s_pe, pe_y), hist(y_res, ky - 2, s_dve)],
                     inc=(s_act, 1))
                y_tanh.append(s_act.n)

                def resf(oc=oc, tb=ky % 2):
                    def f(e):
                        return e.scalar_tensor_tensor(
                            env["s_xo"][tb][:], env["s_t1"][tb][:], 0.5,
                            s_x[:, oc * T:(oc + 1) * T], ALU.mult, ALU.add)
                    return f
                stp = env["s_sta"] if ky % 2 == 0 else env["s_stb"]
                wr = [(s_act, y_tanh[-1]), x_ready, (stp, stp.n)]
                p.on("vector", resf(), waits=wr, inc=(s_dve, 1))
                y_res.append(s_dve.n)

                p.dma("sync",
                      dst[b].rearrange("(c p) s -> c p s", p=128)[oc, :, t0:t0 + T],
                      env["s_xo"][ky % 2][:], waits=[(s_dve, y_res[-1])], sem=stp)
                st_hist.append((stp, stp.n))


def build_progA(upto=3, no_moe=False, dbg_tiles=()):
    nc = bass.Bass("TRN2", target_bir_lowering=False, debug=False)
    NB = B // NCORES
    H = 16

    xin = nc.dram_tensor("xin", [NB, D, S], F32, kind="ExternalInput").ap()
    xout = nc.dram_tensor("xout", [NB, D, S], F32, kind="ExternalOutput").ap()
    x1scr = nc.dram_tensor("x1scr", [NB, D, S], F32, kind="Internal").ap()
    x2scr = nc.dram_tensor("x2scr", [NB, D, S], F32, kind="Internal").ap()
    ctxscr = nc.dram_tensor("ctxscr", [D, S], BF16, kind="Internal").ap()
    dbg = (nc.dram_tensor("dbg", [len(dbg_tiles), 128, 1184], F32,
                          kind="ExternalOutput").ap() if dbg_tiles else None)
    dbg2 = None
    win = {}
    for n, shape in [("wq", [128, 8 * D]), ("wk", [128, 8 * D]),
                     ("wv", [128, 8 * D]), ("wo", [128, 8 * D]),
                     ("bw", [128, 8 * D]), ("la", [128, 1024]),
                     ("lb", [128, 1024]), ("gw", [128, 80]), ("b16", [8, 128]),
                     ("idf", [128, 128]), ("id16", [16, 16]),
                     ("bq", [128, 8]), ("bk", [128, 8]), ("bv", [128, 8]),
                     ("bo", [128, 8]), ("bb", [128, 8]),
                     ("gnw", [64, 16]), ("gnb", [64, 16]),
                     ("lq1", [16, 64]), ("lk1", [16, 64]),
                     ("lq2", [16, 64]), ("lk2", [16, 64])]:
        win[n] = nc.dram_tensor(n + "_in", shape, F32, kind="ExternalInput").ap()

    with ExitStack() as ctx:
        E = ctx.enter_context
        env = {}
        for n in ["wq", "wk", "wv", "wo", "bw"]:
            env["s_" + n] = E(nc.sbuf_tensor("s_" + n, [128, 8 * D], BF16))
        env["s_la"] = E(nc.sbuf_tensor("s_la", [128, 1024], BF16))
        env["s_lb"] = E(nc.sbuf_tensor("s_lb", [128, 1024], BF16))
        env["s_gw"] = E(nc.sbuf_tensor("s_gw", [128, 80], BF16))
        env["s_b16"] = E(nc.sbuf_tensor("s_b16", [8, 128], BF16))
        env["s_idf"] = E(nc.sbuf_tensor("s_idf", [128, 128], F32))
        s_id16 = E(nc.sbuf_tensor("s_id16", [16, 16], F32))
        for n in ["bq", "bk", "bv", "bo", "bb"]:
            env["s_" + n] = E(nc.sbuf_tensor("s_" + n, [128, 8], F32))
        s_gnw = E(nc.sbuf_tensor("s_gnw", [64, 16], F32))
        s_gnb = E(nc.sbuf_tensor("s_gnb", [64, 16], F32))
        s_lqk = {n: E(nc.sbuf_tensor("s_" + n, [16, 64], F32))
                 for n in ["lq1", "lk1", "lq2", "lk2"]}
        s_lamb = E(nc.sbuf_tensor("s_lamb", [128, 16], F32))
        r_lam = E(nc.sbuf_tensor("r_lam", [1, 16], F32))
        s_lt = E(nc.sbuf_tensor("s_lt", [16, 64 + 8], F32))
        ones_col_bf = E(nc.sbuf_tensor("ones_col_bf", [128, 1], BF16))
        ones_row_f = E(nc.sbuf_tensor("ones_row_f", [1, 128], F32))
        ones65 = E(nc.sbuf_tensor("ones65", [65, 64], F32))
        env["s_x"] = E(nc.sbuf_tensor("s_x", [128, CD * T], F32))
        env["s_xbf"] = [E(nc.sbuf_tensor("s_xbf", [128, CD * T], BF16))]
        s_xsq = [E(nc.sbuf_tensor("s_xsq", [128, CD * T], BF16))]
        env["s_asb"] = E(nc.sbuf_tensor("s_asb", [128, T], F32))
        env["s_bsb"] = E(nc.sbuf_tensor("s_bsb", [128, T], F32))
        r_rows = E(nc.sbuf_tensor("r_rows", [1, 3 * T], F32))
        s_ntmpA = E(nc.sbuf_tensor("s_ntmpA", [128, T], F32))
        env["s_lg"] = E(nc.sbuf_tensor("s_lg", [128, 40], F32))
        env["s_gi"] = E(nc.sbuf_tensor("s_gi", [128, 80], F32))
        env["s_gl"] = E(nc.sbuf_tensor("s_gl", [128, 40], F32))
        env["s_gs"] = E(nc.sbuf_tensor("s_gs", [128, 64], F32))
        env["s_ewf"] = E(nc.sbuf_tensor("s_ewf", [128, 32], F32))
        env["s_ewt"] = E(nc.sbuf_tensor("s_ewt", [8, 512], BF16))
        env["s_zf"] = E(nc.sbuf_tensor("s_zf", [128, T], BF16))
        env["s_zw"] = E(nc.sbuf_tensor("s_zw", [128, T], BF16))
        env["s_t1"] = [E(nc.sbuf_tensor(f"s_t1{i}", [128, T], F32)) for i in range(2)]
        env["s_xo"] = [E(nc.sbuf_tensor(f"s_xo{i}", [128, T], F32)) for i in range(2)]
        s_xnb = E(nc.sbuf_tensor("s_xnb", [128, CD * S], BF16))
        s_q = E(nc.sbuf_tensor("s_q", [128, S], BF16))
        s_k = E(nc.sbuf_tensor("s_k", [128, S], BF16))
        s_vt = E(nc.sbuf_tensor("s_vt", [128, S], BF16))
        s_vtok = E(nc.sbuf_tensor("s_vtok", [128, 16 * 160], BF16))
        s_P = [E(nc.sbuf_tensor(f"s_P{i}", [128, 1024], BF16)) for i in range(2)]
        s_ctxh = E(nc.sbuf_tensor("s_ctxh", [64, 2 * S], BF16))
        s_sq2 = s_vt[0:64, :]
        s_cf_t = None
        s_rd = E(nc.sbuf_tensor("s_rd", [65, T], F32))
        s_gns = E(nc.sbuf_tensor("s_gns", [64, 32], F32))
        s_gnsb = E(nc.sbuf_tensor("s_gnsb", [64, 2], BF16))
        s_ga = E(nc.sbuf_tensor("s_ga", [64, 16], F32))
        s_gb = E(nc.sbuf_tensor("s_gb", [64, 16], F32))
        s_gt = None
        r_gn = r_rows[0:1, 0:8]
        s_nrm = s_xnb

        psP1 = E(nc.psum_tensor("psP1", [128, 2048], F32))
        psP2 = E(nc.psum_tensor("psP2", [128, 1536], F32))
        psC = E(nc.psum_tensor("psC", [128, 512], F32))
        env.update(psP1=psP1, psP2=psP2, psC=psC)

        s_lw = Sem(ctx, nc, "s_lw")
        s_lx = Sem(ctx, nc, "s_lx")
        s_ln = Sem(ctx, nc, "s_ln")
        s_pe = Sem(ctx, nc, "s_pe")
        s_act = Sem(ctx, nc, "s_act")
        s_dve = Sem(ctx, nc, "s_dve")
        s_st = Sem(ctx, nc, "s_st")
        s_sta = Sem(ctx, nc, "s_sta")
        s_stb = Sem(ctx, nc, "s_stb")
        s_stc = Sem(ctx, nc, "s_stc")
        s_stv = Sem(ctx, nc, "s_stv")
        s_dbg = Sem(ctx, nc, "s_dbg")
        env["dbg_tiles"] = dbg_tiles
        env["dbg"] = dbg
        env["dbg_live"] = True
        env.update(s_lw=s_lw, s_lx=s_lx, s_ln=s_ln, s_pe=s_pe, s_act=s_act,
                   s_dve=s_dve, s_st=s_st, s_sta=s_sta, s_stb=s_stb,
                   s_stc=s_stc, s_stv=s_stv, s_dbg=s_dbg, no_moe=no_moe,
                   dbg_hold=[])
        sems3 = {"pe": s_pe, "act": s_act, "dve": s_dve}

        p = Prog(nc)

        def consts(e):
            e.memset(s_vtok[:], 0.0)
            e.memset(ones_col_bf[:], 1.0)
            e.memset(ones_row_f[:], 1.0)
            e.memset(ones65[:], 1.0)
            e.memset(s_gns[:], 0.0)
            for kc in range(16):
                i = e.memset(s_vtok[:, kc * 160 + 64:kc * 160 + 65], 1.0)
                i = e.memset(s_vtok[:, kc * 160 + 144:kc * 160 + 145], 1.0)
            return i
        p.on("vector", consts, inc=(s_dve, 1))
        dve_consts = s_dve.n

        for n in ["wq", "wk", "wv", "wo", "bw", "la", "lb", "gw", "b16"]:
            p.dma("gpsimd", env["s_" + n][:], win[n], sem=s_lw)
        p.dma("sync", env["s_idf"][:], win["idf"], sem=s_st)
        p.dma("sync", s_id16[:], win["id16"], sem=s_st)
        for n in ["bq", "bk", "bv", "bo", "bb"]:
            p.dma("sync", env["s_" + n][:], win[n], sem=s_st)
        p.dma("sync", s_gnw[:], win["gnw"], sem=s_st)
        p.dma("sync", s_gnb[:], win["gnb"], sem=s_st)
        for n in ["lq1", "lk1", "lq2", "lk2"]:
            p.dma("sync", s_lqk[n][:], win[n], sem=s_st)
        env["w_ready"] = s_lw.n
        env["b_ready"] = (s_st, s_st.n)

        p.chain("vector", [
            lambda e: e.tensor_mul(s_lt[:, 0:64], s_lqk["lq1"][:], s_lqk["lk1"][:]),
            lambda e: e.reduce_sum(s_lt[:, 64:65], s_lt[:, 0:64], axis=AX.X),
            lambda e: e.tensor_mul(s_lt[:, 0:64], s_lqk["lq2"][:], s_lqk["lk2"][:]),
            lambda e: e.reduce_sum(s_lt[:, 65:66], s_lt[:, 0:64], axis=AX.X),
        ], s_dve, waits=[(s_lw, env["w_ready"]), env["b_ready"]])
        d1 = s_dve.n

        p.chain("scalar", [
            lambda e: e.activation(s_lt[:, 66:67], s_lt[:, 64:65], AF.Exp),
            lambda e: e.activation(s_lt[:, 67:68], s_lt[:, 65:66], AF.Exp),
        ], s_act, waits=[(s_dve, d1)])
        a1 = s_act.n

        p.chain("vector", [
            lambda e: e.tensor_sub(s_lt[:, 68:69], s_lt[:, 66:67], s_lt[:, 67:68]),
            lambda e: e.tensor_scalar_add(s_lt[:, 70:71], s_lt[:, 68:69], 0.8),
            lambda e: e.tensor_scalar_mul(s_lt[:, 69:70], s_lt[:, 70:71], 0.125),
        ], s_dve, waits=[(s_act, a1)])
        d2 = s_dve.n

        def lam4(e):
            return e.transpose(psC[0:1, 0:16], s_lt[:, 69:70], s_id16[:])
        p.on("tensor", lam4, waits=[(s_dve, d2), (s_lw, env["w_ready"])],
             inc=(s_pe, 1))
        pe1 = s_pe.n

        def lam5(e):
            return e.tensor_copy(r_lam[:], psC[0:1, 0:16])
        p.on("vector", lam5, waits=[(s_pe, pe1)], inc=(s_dve, 1))
        d3 = s_dve.n

        def lam6(e):
            return e.matmul(psC[:, 16:32], ones_row_f[:], r_lam[:],
                            start=True, stop=True)
        p.on("tensor", lam6, waits=[(s_dve, d3)], inc=(s_pe, 1))
        pe2 = s_pe.n

        def lam7(e):
            return e.tensor_copy(s_lamb[:], psC[:, 16:32])
        p.on("vector", lam7, waits=[(s_pe, pe2)], inc=(s_dve, 1))
        dve_lam = s_dve.n

        env["ln_bufs"] = {
            "xbf": env["s_xbf"], "xsq": s_xsq, "rows": r_rows,
            "ones_col_bf": ones_col_bf, "ones_row_f": ones_row_f,
            "ntmp": s_ntmpA, "stat_par": 1,
            "ab_sbuf": (env["s_asb"], env["s_bsb"]),
            "ps_stat": lambda par: (psP1[0:1, 0:T], psP1[0:1, 512:512 + T]),
            "ps_ab": lambda par: (psP1[:, 1024:1024 + T], psP1[:, 1536:1536 + T]),
        }
        env["mk_lnh"] = new_ln_hists
        env["prev_ge"] = None
        env["zw_hist"] = [dve_lam]
        env["y_tanh"] = []
        env["y_res"] = [dve_lam]
        env["st_hist"] = []
        env["ffn_ymm"] = [pe2]
        env["entry"] = lambda: None

        _emit_ffn_sweep(p, sems3, env, xin, x1scr if upto > 1 else xout)
        ffn1_res = env["y_res"][-1]
        ffn1_st = env["st_hist"][-1]
        if upto == 1:
            p.on("sync", lambda e: e.nop(),
                 waits=[(s_sta, s_sta.n), (s_stb, s_stb.n),
                        (s_dbg, s_dbg.n) if dbg_tiles else None])
            with nc.Block() as block:
                p.replay(block)
            return nc

        lhA = new_ln_hists()
        qkv_ev = []
        vtr_hist = []
        pv_hist = [env["ffn_ymm"][-1]]
        exp_hist = []
        s_hist = [None, None]
        ctxn_hist = [env["zw_hist"][-1]]
        ctxst_hist = []
        att_res = env["y_res"]
        att_st = env["st_hist"]
        y_tanh = env["y_tanh"]

        for b in range(NB):
            prev_res = att_res[-1]
            for t in range(4):
                kln = len(lhA["norm"])
                t0 = t * T
                if t == 0:
                    lw = [hist(att_res, len(att_res) - 1, s_dve)]
                else:
                    lw = [hist(lhA["norm"], kln - 1, s_dve)]
                p.dma("sync", env["s_x"][:].rearrange("p (c s) -> p c s", s=T),
                      xtile_ap(x1scr, b, t0, T), waits=lw, sem=s_lx)
                x_ready = (s_lx, s_lx.n)
                x_chunks = [env["s_x"][:, c * T:(c + 1) * T] for c in range(CD)]
                xn_chunks = [s_xnb[:, c * S + t0:c * S + t0 + T] for c in range(CD)]
                cpx = hist(env["ffn_ymm"], len(env["ffn_ymm"]) - 1, s_pe)
                emit_ln_tile(p, sems3, env["ln_bufs"], x_chunks, xn_chunks,
                             x_ready, lhA, cp_extra=cpx)
            ln_done = lhA["norm"][-1]

            for hp in range(8):
                for qt in range(4):
                    q0 = qt * T

                    def qkvmm(hp=hp, q0=q0):
                        def f(e):
                            for c in range(CD):
                                e.matmul(psP2[:, 0:T],
                                         env["s_wq"][:, c * D + hp * 128:c * D + (hp + 1) * 128],
                                         s_xnb[:, c * S + q0:c * S + q0 + T],
                                         start=(c == 0), stop=(c == CD - 1))
                            for c in range(CD):
                                e.matmul(psP2[:, 512:512 + T],
                                         env["s_wk"][:, c * D + hp * 128:c * D + (hp + 1) * 128],
                                         s_xnb[:, c * S + q0:c * S + q0 + T],
                                         start=(c == 0), stop=(c == CD - 1))
                            for c in range(CD):
                                i = e.matmul(psP2[:, 1024:1024 + T],
                                             env["s_wv"][:, c * D + hp * 128:c * D + (hp + 1) * 128],
                                             s_xnb[:, c * S + q0:c * S + q0 + T],
                                             start=(c == 0), stop=(c == CD - 1))
                            return i
                        return f
                    wq = [(s_dve, ln_done), hist(qkv_ev, len(qkv_ev) - 1, s_act),
                          (s_dve, prev_res)]
                    if hp > 0:
                        wq.append((s_dve, ctxn_hist[-1]))
                    p.on("tensor", qkvmm(), waits=wq, inc=(s_pe, 1))
                    pe_qkv = s_pe.n

                    def qkvev(hp=hp, q0=q0):
                        def f(e):
                            e.activation(s_q[:, q0:q0 + T], psP2[:, 0:T],
                                         AF.Identity, bias=env["s_bq"][:, hp:hp + 1])
                            e.activation(s_k[:, q0:q0 + T], psP2[:, 512:512 + T],
                                         AF.Identity, bias=env["s_bk"][:, hp:hp + 1])
                            return e.activation(s_vt[:, q0:q0 + T],
                                                psP2[:, 1024:1024 + T], AF.Identity,
                                                bias=env["s_bv"][:, hp:hp + 1])
                        return f
                    wv_ = [(s_pe, pe_qkv)]
                    if len(vtr_hist) >= 1:
                        wv_.append((s_stv, vtr_hist[-1]))
                    p.on("scalar", qkvev(), waits=wv_, inc=(s_act, 1))
                    qkv_ev.append(s_act.n)

                qkv_done = qkv_ev[-1]

                def vtr(hp=hp):
                    def f(e):
                        for kc in range(16):
                            e.dma_start_transpose(
                                s_vtok[:, kc * 160:kc * 160 + 64],
                                s_vt[0:64, kc * 128:(kc + 1) * 128]).then_inc(s_stv.h, 16)
                            i = e.dma_start_transpose(
                                s_vtok[:, kc * 160 + 80:kc * 160 + 144],
                                s_vt[64:128, kc * 128:(kc + 1) * 128])
                            i.then_inc(s_stv.h, 16)
                        return i
                    return f
                wtr = [(s_act, qkv_done), hist(pv_hist, len(pv_hist) - 1, s_pe)]
                s_stv.n += 32 * 16
                p.on("sync", vtr(), waits=wtr)
                vtr_hist.append(s_stv.n)
                vtr_done = s_stv.n


                for ho in range(2):
                    h = 2 * hp + ho
                    hb = ho * 64
                    for qt in range(4):
                        q0 = qt * T
                        for g in range(8):
                            kg = len(exp_hist)
                            sof = (g % 2) * 1024

                            def smm(hp=hp, hb=hb, q0=q0, g=g, sof=sof):
                                def f(e):
                                    for j in range(2):
                                        kc = g * 2 + j
                                        i = e.matmul(
                                            psP1[:, sof + j * 512:sof + (j + 1) * 512],
                                            s_k[hb:hb + 64, kc * 128:(kc + 1) * 128],
                                            s_q[hb:hb + 64, q0:q0 + T],
                                            start=True, stop=True)
                                    return i
                                return f
                            ws = [hist(exp_hist, kg - 2, s_act)]
                            if g == 0:
                                ws.append((s_act, qkv_done))
                            p.on("tensor", smm(), waits=ws, inc=(s_pe, 1))
                            pe_s = s_pe.n

                            def expf(h=h, g=g, sof=sof, pb=g % 2):
                                def f(e):
                                    return e.activation(
                                        s_P[pb][:], psP1[:, sof:sof + 1024],
                                        AF.Exp, scale=s_lamb[:, h:h + 1])
                                return f
                            we = [(s_pe, pe_s), hist(pv_hist, len(pv_hist) - 1, s_pe)] \
                                if g < 2 else [(s_pe, pe_s)]
                            p.on("scalar", expf(), waits=we, inc=(s_act, 1))
                            exp_hist.append(s_act.n)

                            def pvmm(ho=ho, g=g, pb=g % 2):
                                def f(e):
                                    for j in range(2):
                                        kc = g * 2 + j
                                        co = kc * 160 + (0 if ho == 0 else 80)
                                        i = e.matmul(
                                            psC[0:65, 0:T],
                                            s_vtok[:, co:co + 65],
                                            s_P[pb][:, j * 512:(j + 1) * 512],
                                            start=(kc == 0), stop=(kc == 15))
                                    return i
                                return f
                            wp = [(s_act, exp_hist[-1])]
                            if g == 0:
                                wp += [(s_stv, vtr_done),
                                       (s_dve, ctxn_hist[-1]),
                                       (s_stc, s_stc.n)]
                            p.on("tensor", pvmm(), waits=wp, inc=(s_pe, 1))
                            pv_hist.append(s_pe.n)
                        pv_done = pv_hist[-1]

                        def rden(e):
                            return e.reciprocal(s_rd[64:65, :], psC[64:65, 0:T])
                        p.on("vector", rden, waits=[(s_pe, pv_done)], inc=(s_dve, 1))
                        d_rd = s_dve.n

                        def rdbc(e):
                            return e.matmul(psP2[0:64, 0:T], ones65[64:65, :],
                                            s_rd[64:65, :], start=True, stop=True)
                        p.on("tensor", rdbc,
                             waits=[(s_dve, d_rd), (s_dve, ctxn_hist[-1])],
                             inc=(s_pe, 1))
                        pe_rd = s_pe.n

                        def cfev(e):
                            return e.activation(env["s_t1"][0][0:64, :],
                                                psC[0:64, 0:T], AF.Copy)
                        p.on("scalar", cfev, waits=[(s_pe, pv_done)], inc=(s_act, 1))
                        a_cf = s_act.n

                        def ctxn(ho=ho, q0=q0):
                            def f(e):
                                return e.tensor_mul(
                                    s_ctxh[:, ho * S + q0:ho * S + q0 + T],
                                    env["s_t1"][0][0:64, :], psP2[0:64, 0:T])
                            return f
                        wc = [(s_act, a_cf), (s_pe, pe_rd), (s_stc, s_stc.n)]
                        p.on("vector", ctxn(), waits=wc, inc=(s_dve, 1))
                        ctxn_hist.append(s_dve.n)

                    def gsum(ho=ho, h=h):
                        def f(e):
                            return e.reduce_sum(s_gns[:, h:h + 1],
                                                s_ctxh[:, ho * S:(ho + 1) * S],
                                                axis=AX.X)
                        return f
                    p.on("vector", gsum(), waits=[(s_dve, ctxn_hist[-1])],
                         inc=(s_dve, 1))

                    def gsq(ho=ho, h=h):
                        def f(e):
                            return e.activation(s_sq2,
                                                s_ctxh[:, ho * S:(ho + 1) * S],
                                                AF.Square,
                                                accum_out=s_gns[:, 16 + h:17 + h])
                        return f
                    p.on("scalar", gsq(), waits=[(s_dve, ctxn_hist[-1])],
                         inc=(s_act, 1))
                    a_sq = s_act.n

                    p.chain("vector", [
                        (lambda h=h: lambda e: e.tensor_copy(
                            s_gnsb[:, 0:1], s_gns[:, h:h + 1]))(),
                        (lambda h=h: lambda e: e.tensor_copy(
                            s_gnsb[:, 1:2], s_gns[:, 16 + h:17 + h]))(),
                    ], s_dve, waits=[(s_act, a_sq)])
                    d_gc = s_dve.n

                    def gmm(e):
                        return e.matmul(psC[0:1, 32:34], ones_col_bf[0:64, :],
                                        s_gnsb[:, 0:2], start=True, stop=True)
                    p.on("tensor", gmm, waits=[(s_dve, d_gc)], inc=(s_pe, 1))
                    pe_gm = s_pe.n

                    def grow(e):
                        e.tensor_scalar_mul(r_gn[0:1, 0:1], psC[0:1, 32:33],
                                            1.0 / (S * 64))
                        e.tensor_scalar_mul(r_gn[0:1, 1:2], psC[0:1, 33:34],
                                            1.0 / (S * 64))
                        e.tensor_mul(r_gn[0:1, 2:3], r_gn[0:1, 0:1], r_gn[0:1, 0:1])
                        e.tensor_sub(r_gn[0:1, 1:2], r_gn[0:1, 1:2], r_gn[0:1, 2:3])
                        return e.tensor_scalar_add(r_gn[0:1, 1:2], r_gn[0:1, 1:2],
                                                   EPS)
                    p.on("vector", grow, waits=[(s_pe, pe_gm)], inc=(s_dve, 1))
                    d_gr = s_dve.n

                    def gsqr(e):
                        return e.activation(r_gn[0:1, 3:4], r_gn[0:1, 1:2], AF.Sqrt)
                    p.on("scalar", gsqr, waits=[(s_dve, d_gr)], inc=(s_act, 1))
                    a_gs = s_act.n

                    p.chain("vector", [
                        lambda e: e.reciprocal(r_gn[0:1, 4:5], r_gn[0:1, 3:4]),
                        lambda e: e.tensor_mul(r_gn[0:1, 5:6], r_gn[0:1, 0:1],
                                               r_gn[0:1, 4:5]),
                    ], s_dve, waits=[(s_act, a_gs)])
                    d_gre = s_dve.n

                    def gbc(e):
                        return e.matmul(psC[0:64, 34:36], ones_row_f[0:1, 0:64],
                                        r_gn[0:1, 4:6], start=True, stop=True)
                    p.on("tensor", gbc, waits=[(s_dve, d_gre)], inc=(s_pe, 1))
                    pe_gb = s_pe.n

                    p.chain("vector", [
                        (lambda h=h: lambda e: e.tensor_mul(
                            s_ga[:, h:h + 1], psC[0:64, 34:35],
                            s_gnw[:, h:h + 1]))(),
                        (lambda h=h: lambda e: e.tensor_mul(
                            env["s_t1"][1][0:64, 0:1], psC[0:64, 35:36],
                            s_gnw[:, h:h + 1]))(),
                        (lambda h=h: lambda e: e.tensor_sub(
                            s_gb[:, h:h + 1], s_gnb[:, h:h + 1],
                            env["s_t1"][1][0:64, 0:1]))(),
                    ], s_dve, waits=[(s_pe, pe_gb)])
                    d_gab = s_dve.n

                    def gnorm(ho=ho, h=h):
                        def f(e):
                            return e.tensor_scalar(
                                s_ctxh[:, ho * S:(ho + 1) * S],
                                s_ctxh[:, ho * S:(ho + 1) * S],
                                s_ga[:, h:h + 1], s_gb[:, h:h + 1],
                                ALU.mult, ALU.add)
                        return f
                    p.on("vector", gnorm(), waits=[(s_dve, d_gab)], inc=(s_dve, 1))
                    d_gn = s_dve.n

                    p.dma("sync", ctxscr[h * 64:(h + 1) * 64, :],
                          s_ctxh[:, ho * S:(ho + 1) * S],
                          waits=[(s_dve, d_gn)], sem=s_stc)
                    ctxst_hist.append(s_stc.n)
            ctx_all_done = ctxst_hist[-1]

            att_dst = x2scr if upto > 2 else xout
            for t in range(4):
                t0 = t * T
                lwn = [(s_stc, ctx_all_done)]
                if t >= 2:
                    lwn.append(hist(env["ffn_ymm"], len(env["ffn_ymm"]) - 1, s_pe))
                p.dma("sync", s_nrm[:, 0:CD * T].rearrange("p (c s) -> p c s", s=T),
                      ctxscr.rearrange("(c p) s -> p c s", p=128)[:, :, t0:t0 + T],
                      waits=lwn, sem=s_ln)
                nrm_rdy = s_ln.n
                lwx = [hist(att_res, len(att_res) - 1, s_dve)]
                p.dma("sync", env["s_x"][:].rearrange("p (c s) -> p c s", s=T),
                      xtile_ap(x1scr, b, t0, T), waits=lwx, sem=s_lx)
                x_rdy = (s_lx, s_lx.n)

                for oc in range(CD):
                    ky = len(y_tanh)
                    pof = (ky % 2) * 512

                    def ymm(oc=oc, pof=pof):
                        def f(e):
                            for c in range(CD):
                                i = e.matmul(psP2[:, pof:pof + T],
                                             env["s_wo"][:, c * D + oc * 128:c * D + (oc + 1) * 128],
                                             s_nrm[:, c * T:(c + 1) * T],
                                             start=(c == 0), stop=(c == CD - 1))
                            return i
                        return f
                    wy = [(s_ln, nrm_rdy), hist(y_tanh, ky - 2, s_act)]
                    if t == 0 and oc == 0:
                        wy.append((s_dve, ctxn_hist[-1]))
                    p.on("tensor", ymm(), waits=wy, inc=(s_pe, 1))
                    env["ffn_ymm"].append(s_pe.n)
                    pe_y = s_pe.n

                    def t1f(oc=oc, pof=pof, tb=ky % 2):
                        def f(e):
                            return e.activation(env["s_t1"][tb][:],
                                                psP2[:, pof:pof + T], AF.Identity,
                                                bias=env["s_bo"][:, oc:oc + 1])
                        return f
                    p.on("scalar", t1f(),
                         waits=[(s_pe, pe_y), hist(att_res, len(att_res) - 2, s_dve)],
                         inc=(s_act, 1))
                    y_tanh.append(s_act.n)

                    def resf(oc=oc, tb=ky % 2):
                        def f(e):
                            return e.tensor_add(env["s_xo"][tb][:],
                                                env["s_t1"][tb][:],
                                                env["s_x"][:, oc * T:(oc + 1) * T])
                        return f
                    stp = env["s_sta"] if ky % 2 == 0 else env["s_stb"]
                    wr = [(s_act, y_tanh[-1]), x_rdy, (stp, stp.n)]
                    p.on("vector", resf(), waits=wr, inc=(s_dve, 1))
                    att_res.append(s_dve.n)

                    p.dma("sync",
                          att_dst[b].rearrange("(c p) s -> c p s", p=128)[oc, :, t0:t0 + T],
                          env["s_xo"][ky % 2][:], waits=[(s_dve, att_res[-1])],
                          sem=stp)
                    att_st.append((stp, stp.n))

        env["dbg_live"] = False
        if upto > 2:
            _emit_ffn_sweep(p, sems3, env, x2scr, xout)

        p.on("sync", lambda e: e.nop(),
             waits=[(s_sta, s_sta.n), (s_stb, s_stb.n)])

        with nc.Block() as block:
            p.replay(block)

    return nc


def prep_progA_weights(Wq, bq, Wk, bk, Wv, bv, Wo, bo, lq1, lk1, lq2, lk2,
                       gn_w, gn_b, base_w, base_b, gate_w, lora_A, lora_B):
    b16 = np.zeros((8, 128), np.float32)
    for e in range(8):
        b16[e, e * 16:(e + 1) * 16] = 1.0
    return {
        "wq_in": chunk_major(Wq), "wk_in": chunk_major(Wk),
        "wv_in": chunk_major(Wv), "wo_in": chunk_major(Wo),
        "bw_in": chunk_major(np.ascontiguousarray(base_w.T)),
        "la_in": chunk_major(np.ascontiguousarray(
            lora_A.transpose(2, 0, 1).reshape(1024, 128))),
        "lb_in": chunk_major(np.ascontiguousarray(
            lora_B.transpose(0, 2, 1).reshape(128, 1024))),
        "gw_in": chunk_major(np.ascontiguousarray(gate_w.T)),
        "b16_in": b16,
        "idf_in": np.eye(128, dtype=np.float32),
        "id16_in": np.eye(16, dtype=np.float32),
        "bq_in": col8(bq), "bk_in": col8(bk), "bv_in": col8(bv),
        "bo_in": col8(bo), "bb_in": col8(base_b),
        "gnw_in": np.ascontiguousarray(gn_w.T), "gnb_in": np.ascontiguousarray(gn_b.T),
        "lq1_in": lq1, "lk1_in": lk1, "lq2_in": lq2, "lk2_in": lk2,
    }


def run_progA(x_fm, weights, nc=None, upto=3, want_dbg=False):
    if nc is None:
        nc = build_progA(upto)
    nb = B // NCORES
    in_maps = []
    for c in range(NCORES):
        m = {"xin": np.ascontiguousarray(x_fm[c * nb:(c + 1) * nb])}
        m.update(weights)
        in_maps.append(m)
    res = run_bass_kernel_spmd(nc, in_maps, core_ids=list(range(NCORES)))
    out = np.concatenate([r["xout"] for r in res.results], axis=0)
    if want_dbg:
        return out, [r.get("dbg") for r in res.results]
    return out



_NC_CACHE = {}


def _get_nc(name, builder):
    if name not in _NC_CACHE:
        _NC_CACHE[name] = builder()
    return _NC_CACHE[name]


def kernel(L0_attn_x, L0_attn_Wq, L0_attn_bq, L0_attn_Wk, L0_attn_bk,
           L0_attn_Wv, L0_attn_bv, L0_attn_Wo, L0_attn_bo,
           L0_attn_lq1, L0_attn_lk1, L0_attn_lq2, L0_attn_lk2,
           L0_attn_gn_w, L0_attn_gn_b,
           L0_ffn_base_w, L0_ffn_base_b, L0_ffn_gate_w, L0_ffn_lora_A,
           L0_ffn_lora_B,
           L1_attn_W1, L1_attn_b1, L1_attn_w2, L1_attn_b2, L1_attn_Wc,
           L1_attn_bc,
           L1_ffn_wf, L1_ffn_uf, L1_ffn_bf, L1_ffn_wi, L1_ffn_ui, L1_ffn_bi,
           L1_ffn_wc, L1_ffn_uc, L1_ffn_bc, L1_ffn_wo, L1_ffn_uo, L1_ffn_bo,
           L1_ffn_in_c):
    asnp = lambda a: np.asarray(a, dtype=np.float32)
    wA = prep_progA_weights(asnp(L0_attn_Wq), asnp(L0_attn_bq),
                            asnp(L0_attn_Wk), asnp(L0_attn_bk),
                            asnp(L0_attn_Wv), asnp(L0_attn_bv),
                            asnp(L0_attn_Wo), asnp(L0_attn_bo),
                            asnp(L0_attn_lq1), asnp(L0_attn_lk1),
                            asnp(L0_attn_lq2), asnp(L0_attn_lk2),
                            asnp(L0_attn_gn_w), asnp(L0_attn_gn_b),
                            asnp(L0_ffn_base_w), asnp(L0_ffn_base_b),
                            asnp(L0_ffn_gate_w), asnp(L0_ffn_lora_A),
                            asnp(L0_ffn_lora_B))
    wB = prep_progB_weights(asnp(L1_ffn_wf), asnp(L1_ffn_uf), asnp(L1_ffn_bf),
                            asnp(L1_ffn_wi), asnp(L1_ffn_ui), asnp(L1_ffn_bi),
                            asnp(L1_ffn_wc), asnp(L1_ffn_uc), asnp(L1_ffn_bc),
                            asnp(L1_ffn_wo), asnp(L1_ffn_uo), asnp(L1_ffn_bo),
                            asnp(L1_ffn_in_c))
    wC = prep_progC_weights(asnp(L1_attn_W1), asnp(L1_attn_b1),
                            asnp(L1_attn_w2), asnp(L1_attn_b2),
                            asnp(L1_attn_Wc), asnp(L1_attn_bc))

    ncA = _get_nc("A", build_progA)
    ncB = _get_nc("B", build_progB)
    ncC = _get_nc("C", build_progC)

    x_fm = np.ascontiguousarray(asnp(L0_attn_x).transpose(0, 2, 1))
    xa = run_progA(x_fm, wA, nc=ncA)
    xb = run_progB(xa, wB, nc=ncB)
    xc = run_progC(xb, wC, nc=ncC)
    xd = run_progB(xc, wB, nc=ncB)
    return np.ascontiguousarray(xd.transpose(0, 2, 1))
